# version 04
# baseline (speedup 1.0000x reference)
import numpy as np
from contextlib import ExitStack

import concourse.bass as bass
import concourse.mybir as mybir
from concourse.bass_utils import run_bass_kernel_spmd

F32 = mybir.dt.float32
BF16 = mybir.dt.bfloat16
AF = mybir.ActivationFunctionType
ALU = mybir.AluOpType
AX = mybir.AxisListType

N_CORES = 8


def make_cfg(S=2048, D=1024, H=16, F=4096, NB=2):
    dk = D // H
    assert dk == 64 and D % 128 == 0 and S % 512 == 0 and F % 256 == 0 and H % 2 == 0
    return dict(
        S=S, D=D, H=H, F=F, NB=NB, dk=dk,
        DC=D // 128, QT=S // 512, KC=S // 128,
        FB=F // 256, TB=S // 128, HP=H // 2,
        REL=2 * 512 - 1, CLIP=2 * 512 - 2,
    )


def band_map(cfg):
    S, CLIP = cfg["S"], cfg["CLIP"]
    m = {}
    for qt in range(cfg["QT"]):
        for kc in range(cfg["KC"]):
            base = S - 1 + qt * 512 - kc * 128
            if base - 127 < CLIP:
                m[(kc, qt)] = base
    return m, sorted(set(m.values()))


def pack_weight_strips_qkv(W, D):
    DC = D // 128
    HPn = W.shape[1] // 128
    w = W.reshape(DC, 128, HPn, 128)
    return np.ascontiguousarray(w.transpose(2, 1, 0, 3))


def pack_wfc_strips(Wfc, D, F):
    DC = D // 128
    w = Wfc.reshape(F // 256, 2, 128, DC, 128)
    return np.ascontiguousarray(w.transpose(0, 1, 4, 3, 2))


def pack_wpr_strips(Wpr, D, F):
    DC = D // 128
    w = Wpr.reshape(DC, 128, F // 256, 2, 128)
    return np.ascontiguousarray(w.transpose(2, 0, 4, 3, 1))


def pack_bias_cols(b, D):
    return np.ascontiguousarray(b.reshape(D // 128, 128).T)


def pack_bands(rel, cfg):
    _, bases = band_map(cfg)
    CLIP = cfg["CLIP"]
    jj = np.arange(128)[:, None]
    ii = np.arange(512)[None, :]
    out = np.empty((cfg["H"], max(len(bases), 1), 128, 512), dtype=np.float32)
    for bi, base in enumerate(bases):
        idx = np.minimum(base + ii - jj, CLIP)
        out[:, bi] = rel[idx, :].transpose(2, 0, 1)
    return np.ascontiguousarray(out)


class Chan:
    __slots__ = ("name", "sem", "count")

    def __init__(self, name, sem):
        self.name, self.sem, self.count = name, sem, 0


class Prog:
    ENGINES = ("t", "v", "s", "g", "y")

    def __init__(self):
        self.ops = []
        self.wr = {e: set() for e in self.ENGINES}
        self.rd = {e: set() for e in self.ENGINES}
        self.n_drain = 0

    def op(self, eng, fn, waits=(), inc=None, inc_amt=1, reads=(), writes=()):
        for ch, val in waits:
            assert val <= ch.count, (ch.name, val, ch.count)
        if eng in ("v", "s", "g"):
            r, w = set(reads), set(writes)
            if (r | w) & self.wr[eng] or (w & self.rd[eng]):
                self.ops.append((eng, "drain", None))
                self.wr[eng].clear()
                self.rd[eng].clear()
                self.n_drain += 1
            self.rd[eng] |= r
            self.wr[eng] |= w
        self.ops.append((eng, "op", (fn, tuple(waits), inc, inc_amt)))
        if inc is not None:
            inc.count += inc_amt
        return inc.count if inc is not None else None

    def emit(self, eng_key, eng):
        done = {}
        for e, kind, payload in self.ops:
            if e != eng_key:
                continue
            if kind == "drain":
                eng.drain()
                continue
            fn, waits, inc, inc_amt = payload
            for ch, val in waits:
                if val > 0 and done.get(id(ch), -1) < val:
                    eng.wait_ge(ch.sem, val)
                    done[id(ch)] = val
            instr = fn(eng)
            if inc is not None:
                instr.then_inc(inc.sem, inc_amt)


def build_program(cfg):
    S, D, H, F, NB = cfg["S"], cfg["D"], cfg["H"], cfg["F"], cfg["NB"]
    DC, QT, KC, FB, TB, HP = (cfg[k] for k in ("DC", "QT", "KC", "FB", "TB", "HP"))
    PH = cfg.get("ph", 99)
    bmap, bases = band_map(cfg)
    base_idx = {v: i for i, v in enumerate(bases)}
    NBASE = max(len(bases), 1)

    nc = bass.Bass("TRN2", target_bir_lowering=False, debug=False)

    xT = nc.dram_tensor("xT", [D, NB * S], F32, kind="ExternalInput").ap()
    wq = nc.dram_tensor("wq", [HP, 128, DC, 128], F32, kind="ExternalInput").ap()
    wk = nc.dram_tensor("wk", [HP, 128, DC, 128], F32, kind="ExternalInput").ap()
    wv = nc.dram_tensor("wv", [HP, 128, DC, 128], F32, kind="ExternalInput").ap()
    wo = nc.dram_tensor("wo", [DC, 128, DC, 128], F32, kind="ExternalInput").ap()
    wfc = nc.dram_tensor("wfc", [FB, 2, 128, DC, 128], F32, kind="ExternalInput").ap()
    wpr = nc.dram_tensor("wpr", [FB, DC, 128, 2, 128], F32, kind="ExternalInput").ap()
    bqc = nc.dram_tensor("bqc", [128, DC], F32, kind="ExternalInput").ap()
    bkc = nc.dram_tensor("bkc", [128, DC], F32, kind="ExternalInput").ap()
    bvc = nc.dram_tensor("bvc", [128, DC], F32, kind="ExternalInput").ap()
    boc = nc.dram_tensor("boc", [128, DC], F32, kind="ExternalInput").ap()
    crep = nc.dram_tensor("crep", [128, H], F32, kind="ExternalInput").ap()
    bands = nc.dram_tensor("bands", [H, NBASE, 128, 512], F32, kind="ExternalInput").ap()
    ident = nc.dram_tensor("ident", [128, 128], F32, kind="ExternalInput").ap()
    out = nc.dram_tensor("out", [NB * S, D], F32, kind="ExternalOutput").ap()
    dbg = cfg.get("dbg")
    if dbg:
        d_u = nc.dram_tensor("d_u", [128, DC, S], F32, kind="ExternalOutput").ap()
        d_qhi = nc.dram_tensor("d_qhi", [128, S], F32, kind="ExternalOutput").ap()
        d_qlo = nc.dram_tensor("d_qlo", [128, S], F32, kind="ExternalOutput").ap()
        d_khi = nc.dram_tensor("d_khi", [128, S], F32, kind="ExternalOutput").ap()
        d_klo = nc.dram_tensor("d_klo", [128, S], F32, kind="ExternalOutput").ap()
        d_e = nc.dram_tensor("d_e", [128, 2, 512], F32, kind="ExternalOutput").ap()
        d_e2 = nc.dram_tensor("d_e2", [128, 2, 512], F32, kind="ExternalOutput").ap()
        d_ups = nc.dram_tensor("d_ups", [128, 2, 512], F32, kind="ExternalOutput").ap()
        d_y = nc.dram_tensor("d_y", [128, DC, S], F32, kind="ExternalOutput").ap()

    with ExitStack() as ctx:
        def sb(name, shape, dt=F32):
            return ctx.enter_context(nc.sbuf_tensor(name, shape, dt))

        def psb(name):
            return ctx.enter_context(nc.psum_tensor(name, [128, 512], F32))

        c_ident = sb("c_ident", [128, 128])
        c_crep = sb("c_crep", [128, H])
        c_bq = sb("c_bq", [128, DC]); c_bk = sb("c_bk", [128, DC])
        c_bv = sb("c_bv", [128, DC]); c_bo = sb("c_bo", [128, DC])
        onescol = sb("onescol", [128, 1])
        onesrow = sb("onesrow", [1, 128])
        xf = [sb(f"xf{i}", [128, 512]) for i in range(2)]
        xhi = sb("xhi", [128, DC, S], BF16)
        xlo = sb("xlo", [128, DC, S], BF16)
        wf32 = [sb(f"wf32_{i}", [128, DC * 128]) for i in range(2)]
        whi = [sb(f"whi{i}", [128, DC * 128], BF16) for i in range(2)]
        wlo = [sb(f"wlo{i}", [128, DC * 128], BF16) for i in range(2)]
        bbuf = [sb(f"bbuf{i}", [128, 512]) for i in range(2)]
        bscr = [sb(f"bscr{i}", [128, 512]) for i in range(2)]
        qhi = sb("qhi", [128, S], BF16); qlo = sb("qlo", [128, S], BF16)
        khi = sb("khi", [128, S], BF16); klo = sb("klo", [128, S], BF16)
        vtiles = sb("vtiles", [128, KC, 130])
        eA = [sb(f"eA{i}", [128, 512]) for i in range(2)]
        eB = [sb(f"eB{i}", [128, 512]) for i in range(2)]
        Ur = sb("Ur", [128, DC, S])
        rscr = [sb(f"rscr{i}", [128, 512]) for i in range(2)]
        tm = [sb(f"tm{i}", [128, D]) for i in range(2)]
        ob = [sb("ob0", [128, D])]
        nsum = sb("nsum", [128, 1]); negm = sb("negm", [128, 1])
        m2c = sb("m2c", [128, 1]); r2c = sb("r2c", [128, 1])
        sqv = sb("sqv", [128, 1]); ssq = sb("ssq", [128, 1])
        vepsc = sb("vepsc", [128, 1])

        ps = [psb(f"ps{i}") for i in range(8)]

        def chan(name):
            return Chan(name, ctx.enter_context(nc.semaphore(name)))

        cT = chan("cT"); cV = chan("cV"); cS = chan("cS")
        dbg_chan = [chan("cDB")] if cfg.get("dbg") else [None]
        cDC = chan("cDC")
        cW = [chan("cW0"), chan("cW1")]
        cB = [chan("cB0"), chan("cB1")]
        cX = [chan("cX0"), chan("cX1")]
        cO = [chan("cO0"), chan("cO1")]

        P = Prog()

        def mm(e, o, l, r, st, sp, tp=None):
            return e.matmul(o, l, r, start=st, stop=sp, tile_position=tp)

        for dst, src in ((c_ident, ident), (c_crep, crep), (c_bq, bqc),
                         (c_bk, bkc), (c_bv, bvc), (c_bo, boc)):
            P.op("y", lambda e, d=dst, s=src: e.dma_start(out=d[:], in_=s[:]),
                 inc=cDC, inc_amt=16)
        snap_consts = cDC.count
        P.op("v", lambda e: e.memset(onescol[:], 1.0), writes=("onescol",))
        P.op("v", lambda e: e.memset(onesrow[:], 1.0), writes=("onesrow",))
        P.op("v", lambda e: e.memset(
            vtiles.ap().rearrange("p k (h f) -> p k h f", f=65)[:, :, :, 64:65], 1.0),
            inc=cV, writes=("vtiles",))

        def split_pair(dst_hi, dst_lo, src, hi_tag, lo_tag, src_tag,
                       waits=(), inc=None):
            P.op("v", lambda e: e.tensor_copy(dst_hi, src), waits=waits,
                 reads=(src_tag,), writes=(hi_tag,))
            return P.op("v",
                        lambda e: e.tensor_tensor(dst_lo, src, dst_hi, ALU.subtract),
                        reads=(src_tag, hi_tag), writes=(lo_tag,), inc=inc)

        strip_mm = {0: 0, 1: 0}
        strip_split = {0: 0, 1: 0}
        sidx = [0]

        def load_strip(src_ap, ncols, do_split):
            par = sidx[0] % 2
            sidx[0] += 1
            w = []
            if strip_mm[par]:
                w.append((cT, strip_mm[par]))
            if strip_split[par]:
                w.append((cV, strip_split[par]))
            P.op("y", lambda e: e.dma_start(out=wf32[par][:, 0:ncols], in_=src_ap),
                 waits=w, inc=cW[par], inc_amt=16)
            if do_split:
                w2 = [(cW[par], cW[par].count)]
                if strip_mm[par]:
                    w2.append((cT, strip_mm[par]))
                split_pair(whi[par][:, 0:ncols], wlo[par][:, 0:ncols],
                           wf32[par][:, 0:ncols], f"whi{par}", f"wlo{par}",
                           f"wf32_{par}", waits=w2, inc=cV)
                strip_split[par] = cV.count
            else:
                strip_split[par] = 0
            return par, cW[par].count, cV.count

        for b in range(NB):
            bcol = b * S

            for c in range(DC):
                for t4 in range(QT):
                    pi = (c * QT + t4) % 2
                    P.op("g", lambda e, c=c, t4=t4, pi=pi, bcol=bcol: e.dma_start(
                        out=xf[pi][:],
                        in_=xT[c * 128:(c + 1) * 128,
                              bcol + t4 * 512: bcol + (t4 + 1) * 512]),
                        waits=[(cV, cV.count)], inc=cX[pi], inc_amt=16)
                    split_pair(xhi[:, c, t4 * 512:(t4 + 1) * 512],
                               xlo[:, c, t4 * 512:(t4 + 1) * 512],
                               xf[pi][:], "xhi", "xlo", f"xf{pi}",
                               waits=[(cX[pi], cX[pi].count), (cT, cT.count)], inc=cV)

            band_guard = [0]
            bscr_guard = [0, 0]
            for hp in range(HP if PH >= 1 else 0):
                hA, hB = 2 * hp, 2 * hp + 1
                pq = load_strip(wq[hp].rearrange("p c f -> p (c f)"), DC * 128, True)
                pk = load_strip(wk[hp].rearrange("p c f -> p (c f)"), DC * 128, True)

                for pj, ((par, _, vsnap), dsthi, dstlo, bias_sb, htag, ltag) in \
                        enumerate(((pq, qhi, qlo, c_bq, "qhi", "qlo"),
                                   (pk, khi, klo, c_bk, "khi", "klo"))):
                    if pj == 1:
                        pv_ = load_strip(wv[hp].rearrange("p c f -> p (c f)"),
                                         DC * 128, True)
                    for t in range(QT):
                        bank = ps[t % 2]
                        wts = [(cV, cV.count), (cV, vsnap)]
                        first = True
                        for p_ in range(3):
                            lh = (whi, whi, wlo)[p_][par]
                            rh = (xhi, xlo, xhi)[p_]
                            for c in range(DC):
                                last = (p_ == 2 and c == DC - 1)
                                P.op("t", lambda e, lh=lh, rh=rh, c=c, t=t, bank=bank,
                                     f=first, last=last: mm(
                                         e, bank[:],
                                         lh[:, c * 128:(c + 1) * 128],
                                         rh[:, c, t * 512:(t + 1) * 512],
                                         f, last),
                                     waits=wts if first else (),
                                     inc=cT if last else None)
                                first = False
                                wts = ()
                        strip_mm[par] = cT.count
                        hs = dsthi[:, t * 512:(t + 1) * 512]
                        ls = dstlo[:, t * 512:(t + 1) * 512]
                        P.op("v", lambda e, bank=bank, hs=hs, bb=bias_sb, hp=hp:
                             e.tensor_scalar_add(hs, bank[:], bb[:, hp:hp + 1]),
                             waits=[(cT, cT.count), (cDC, snap_consts)],
                             reads=("psQK",), writes=(htag,))
                        P.op("v", lambda e, bank=bank, hs=hs, ls=ls, bb=bias_sb, hp=hp:
                             e.scalar_tensor_tensor(ls, bank[:], bb[:, hp:hp + 1], hs,
                                                    ALU.add, ALU.subtract),
                             reads=("psQK", htag), writes=(ltag,), inc=cV)

                par, _, vsnap = pv_
                for t in range(QT):
                    bank = ps[4 + t % 2]
                    wts = [(cV, cV.count), (cV, vsnap)]
                    first = True
                    for p_ in range(3):
                        lh = (whi, whi, wlo)[p_][par]
                        rh = (xhi, xlo, xhi)[p_]
                        for c in range(DC):
                            last = (p_ == 2 and c == DC - 1)
                            P.op("t", lambda e, lh=lh, rh=rh, c=c, t=t, bank=bank,
                                 f=first, last=last: mm(
                                     e, bank[:], lh[:, c * 128:(c + 1) * 128],
                                     rh[:, c, t * 512:(t + 1) * 512], f, last),
                                 waits=wts if first else (),
                                 inc=cT if last else None)
                            first = False
                            wts = ()
                    strip_mm[par] = cT.count
                    P.op("v", lambda e, bank=bank, hp=hp: e.tensor_scalar_add(
                        rscr[0][:], bank[:], c_bv[:, hp:hp + 1]),
                        waits=[(cT, cT.count), (cDC, snap_consts)],
                        reads=("psV",), writes=("rscr0",), inc=cV)
                    vsn = cV.count
                    for q4 in range(4):
                        kc = t * 4 + q4
                        tps = ps[6 + q4 % 2]
                        P.op("t", lambda e, tps=tps, q4=q4: e.transpose(
                            tps[:, 0:128], rscr[0][:, q4 * 128:(q4 + 1) * 128],
                            c_ident[:]),
                            waits=[(cV, vsn), (cDC, snap_consts), (cV, cV.count)],
                            inc=cT)
                        dst = vtiles.ap().rearrange(
                            "p k (h f) -> p k h f", f=65)[:, kc, :, 0:64]
                        P.op("v", lambda e, tps=tps, dst=dst: e.tensor_copy(
                            dst, tps.ap().rearrange("p (h f) -> p h f", f=64)[:, 0:2, :]),
                            waits=[(cT, cT.count)], reads=("psVT",),
                            writes=("vtiles",), inc=cV)

                for qt in range(QT if PH >= 2 else 0):
                    qsl = slice(qt * 512, (qt + 1) * 512)
                    uA, uB = ps[4], ps[5]
                    sA, sB = [ps[0], ps[1]], [ps[2], ps[3]]
                    s_snaps, exp_snaps, pv_snaps = {}, {}, {}

                    def issue_scores(kc):
                        wts = [(cV, cV.count)] if kc == 0 else []
                        par = kc % 2
                        for p_ in range(3):
                            lhK = (khi, khi, klo)[p_]
                            rhQ = (qhi, qlo, qhi)[p_]
                            for bank, r0 in ((sA[par], 0), (sB[par], 64)):
                                P.op("t", lambda e, lhK=lhK, rhQ=rhQ, bank=bank,
                                     r0=r0, kc=kc, p_=p_, qsl=qsl: mm(
                                         e, bank[:],
                                         lhK[r0:r0 + 64, kc * 128:(kc + 1) * 128],
                                         rhQ[r0:r0 + 64, qsl], p_ == 0, p_ == 2,
                                         tp=(r0, 0)),
                                     waits=wts, inc=cT if p_ == 2 else None)
                                wts = ()
                        s_snaps[kc] = cT.count

                    def prefetch_band(kc):
                        if (kc, qt) not in bmap:
                            return
                        bi = base_idx[bmap[(kc, qt)]]
                        wdma = [(cV, band_guard[0])] if band_guard[0] else []
                        for h_i, h in enumerate((hA, hB)):
                            P.op("y", lambda e, h=h, bi=bi, h_i=h_i: e.dma_start(
                                out=bbuf[h_i][:], in_=bands[h, bi]),
                                waits=wdma, inc=cB[h_i], inc_amt=16)
                            wdma = []

                    def issue_exp(kc):
                        par = kc % 2
                        banded = (kc, qt) in bmap
                        guard = [(cT, pv_snaps[kc - 2])] if kc >= 2 else []
                        for h_i, (h, bank, ebuf) in enumerate(
                                ((hA, sA[par], eA[par]), (hB, sB[par], eB[par]))):
                            if banded:
                                P.op("v", lambda e, bank=bank, h_i=h_i:
                                     e.scalar_tensor_tensor(
                                         bscr[h_i][:], bank[:], 0.125, bbuf[h_i][:],
                                         ALU.mult, ALU.add),
                                     waits=[(cT, s_snaps[kc]),
                                            (cB[h_i], cB[h_i].count),
                                            (cS, bscr_guard[h_i])],
                                     reads=("psS", f"bbuf{h_i}"),
                                     writes=(f"bscr{h_i}",), inc=cV)
                                band_guard[0] = cV.count
                                P.op("s", lambda e, ebuf=ebuf, h_i=h_i: e.activation(
                                    ebuf[:], bscr[h_i][:], AF.Exp),
                                    waits=[(cV, cV.count)] + guard,
                                    reads=(f"bscr{h_i}",), writes=("E",), inc=cS)
                                bscr_guard[h_i] = cS.count
                            else:
                                P.op("s", lambda e, bank=bank, ebuf=ebuf, h=h:
                                     e.activation(ebuf[:], bank[:], AF.Exp,
                                                  bias=c_crep[:, h:h + 1], scale=0.125),
                                     waits=[(cT, s_snaps[kc]),
                                            (cDC, snap_consts)] + guard,
                                     reads=("psS",), writes=("E",), inc=cS)
                        exp_snaps[kc] = cS.count

                    def issue_pv(kc):
                        par = kc % 2
                        for h_i, (ub, ebuf) in enumerate(((uA, eA[par]), (uB, eB[par]))):
                            hsl = slice(h_i * 65, h_i * 65 + 65)
                            P.op("t", lambda e, ub=ub, ebuf=ebuf, kc=kc, hsl=hsl: mm(
                                e, ub[0:65, :], vtiles[:, kc, hsl], ebuf[:],
                                kc == 0, kc == KC - 1),
                                waits=[(cS, exp_snaps[kc])], inc=cT)
                        pv_snaps[kc] = cT.count

                    issue_scores(0)
                    prefetch_band(0)
                    for kc in range(KC):
                        if kc + 1 < KC:
                            issue_scores(kc + 1)
                        issue_exp(kc)
                        if kc + 1 < KC:
                            prefetch_band(kc + 1)
                        issue_pv(kc)

                    if dbg and b == 0 and hp == HP - 1 and qt == cfg.get("dbg_qt", QT - 1):
                        cDB = dbg_chan[0]
                        for h_i, ub in enumerate((uA, uB)):
                            P.op("v", lambda e, ub=ub, h_i=h_i: e.tensor_copy(
                                rscr[h_i][:], ub[:]),
                                 waits=[(cT, pv_snaps[KC - 1]), (cDB, cDB.count),
                                        (cV, cV.count)],
                                 reads=("psU",), writes=(f"rscr{h_i}",), inc=cV)
                            P.op("g", lambda e, h_i=h_i: e.dma_start(
                                out=d_ups[:, h_i, :], in_=rscr[h_i][:]),
                                waits=[(cV, cV.count)], inc=cDB, inc_amt=16)
                        P.op("g", lambda e: e.dma_start(
                            out=d_e2[:, 0, :], in_=eA[(KC - 2) % 2][:]),
                            waits=[(cV, cV.count)], inc=cDB, inc_amt=16)
                        P.op("g", lambda e: e.dma_start(
                            out=d_e2[:, 1, :], in_=eB[(KC - 2) % 2][:]),
                            waits=[(cV, cV.count)], inc=cDB, inc_amt=16)
                        P.op("v", lambda e: e.engine_nop(),
                             waits=[(cDB, cDB.count)],
                             writes=("rscr0", "rscr1"))
                    for h_i, ub in enumerate((uA, uB)):
                        rrow = bscr[h_i][0:1, 0:512]
                        P.op("v", lambda e, ub=ub, rrow=rrow: e.reciprocal(
                            rrow, ub[64:65, :]),
                            waits=[(cT, pv_snaps[KC - 1]), (cS, exp_snaps[KC - 1])],
                            reads=("psU",), writes=(f"bscr{h_i}",), inc=cV)
                        bcb = ps[2 * h_i]
                        P.op("t", lambda e, bcb=bcb, rrow=rrow: mm(
                            e, bcb[0:64, :], onesrow[0:1, 0:64], rrow, True, True),
                            waits=[(cV, cV.count)], inc=cT)
                        bcs = rscr[h_i][0:64, :]
                        P.op("v", lambda e, bcb=bcb, bcs=bcs: e.tensor_copy(
                            bcs, bcb[0:64, :]),
                            waits=[(cT, cT.count)], reads=("psBC",),
                            writes=(f"rscr{h_i}",))
                        dst = Ur[:, hp, qsl][h_i * 64:(h_i + 1) * 64, :]
                        P.op("v", lambda e, ub=ub, bcs=bcs, dst=dst: e.tensor_tensor(
                            dst, ub[0:64, :], bcs, ALU.mult),
                            reads=("psU", f"rscr{h_i}"),
                            writes=("Ur",), inc=cV)
            snap_attn = cV.count
            if dbg and b == 0:
                cDB = dbg_chan[0]
                for nm, src_t, dst_t in (("qhi", qhi, d_qhi), ("qlo", qlo, d_qlo),
                                         ("khi", khi, d_khi), ("klo", klo, d_klo)):
                    for t in range(QT):
                        tsl = slice(t * 512, (t + 1) * 512)
                        P.op("v", lambda e, src_t=src_t, tsl=tsl: e.tensor_copy(
                            rscr[0][:], src_t[:, tsl]),
                            waits=[(cV, cV.count), (cDB, cDB.count)],
                            reads=(nm,), writes=("rscr0",), inc=cV)
                        P.op("g", lambda e, dst_t=dst_t, tsl=tsl: e.dma_start(
                            out=dst_t[:, tsl], in_=rscr[0][:]),
                            waits=[(cV, cV.count)], inc=cDB, inc_amt=16)
                P.op("g", lambda e: e.dma_start(out=d_u[:], in_=Ur[:]),
                     waits=[(cV, cV.count)], inc=cDB, inc_amt=16)
                P.op("g", lambda e: e.dma_start(
                    out=d_e[:, 0, :], in_=eA[(KC - 1) % 2][:]),
                    waits=[(cV, cV.count)], inc=cDB, inc_amt=16)
                P.op("g", lambda e: e.dma_start(
                    out=d_e[:, 1, :], in_=eB[(KC - 1) % 2][:]),
                    waits=[(cV, cV.count)], inc=cDB, inc_amt=16)
                P.op("v", lambda e: e.engine_nop(), waits=[(cDB, cDB.count)],
                     writes=("rscr0",))

            res1_evac = {}
            for t in range(QT if PH >= 3 else 0):
                tsl = slice(t * 512, (t + 1) * 512)
                for o in range(DC):
                    par, dwsnap, _ = load_strip(
                        wo[o].rearrange("p c f -> p (c f)"), DC * 128, False)
                    wts = [(cW[par], dwsnap), (cV, snap_attn)]
                    if t >= 1:
                        wts.append((cV, res1_evac[t - 1]))
                    first = True
                    for c in range(DC):
                        P.op("t", lambda e, c=c, o=o, par=par, tsl=tsl, f=first: mm(
                            e, ps[o][:], wf32[par][:, c * 128:(c + 1) * 128],
                            Ur[:, c, tsl], f, c == DC - 1),
                            waits=wts if first else (),
                            inc=cT if c == DC - 1 else None)
                        first = False
                        wts = ()
                    strip_mm[par] = cT.count
                grp = cT.count
                for o in range(DC):
                    P.op("v", lambda e, o=o, tsl=tsl: e.scalar_tensor_tensor(
                        rscr[o % 2][:], ps[o][:], c_bo[:, o:o + 1], xhi[:, o, tsl],
                        ALU.add, ALU.add),
                        waits=[(cT, grp), (cDC, snap_consts)],
                        reads=("psWo", "xhi"), writes=(f"rscr{o % 2}",))
                    P.op("v", lambda e, o=o, tsl=tsl: e.tensor_tensor(
                        Ur[:, o, tsl], rscr[o % 2][:], xlo[:, o, tsl], ALU.add),
                        reads=(f"rscr{o % 2}", "xlo"), writes=("Ur",), inc=cV)
                res1_evac[t] = cV.count
            snap_res1 = cV.count

            for t in range(QT if PH >= 4 else 0):
                tsl = slice(t * 512, (t + 1) * 512)
                first = True
                for c in range(DC):
                    P.op("t", lambda e, c=c, tsl=tsl, f=first: mm(
                        e, ps[0][0:1, :], onescol[:], Ur[:, c, tsl], f, c == DC - 1),
                        waits=[(cV, snap_res1)] if first else (),
                        inc=cT if c == DC - 1 else None)
                    first = False
                sqmm = {}
                for c in range(DC):
                    w = [(cV, cV.count)]
                    if c >= 2:
                        w.append((cT, sqmm[c - 2]))
                    P.op("s", lambda e, c=c, tsl=tsl: e.activation(
                        rscr[c % 2][:], Ur[:, c, tsl], AF.Square),
                        waits=w, reads=("Ur",), writes=(f"rscr{c % 2}",), inc=cS)
                    P.op("t", lambda e, c=c: mm(
                        e, ps[1][0:1, :], onescol[:], rscr[c % 2][:], c == 0,
                        c == DC - 1),
                        waits=[(cS, cS.count)], inc=cT)
                    sqmm[c] = cT.count
                snap_st = cT.count
                mrow = bscr[0][0:1, 0:512]; vrow = bscr[1][0:1, 0:512]
                rrow = rscr[0][0:1, 0:512]; mrrow = rscr[1][0:1, 0:512]
                msqr = eA[1][0:1, 0:512]; veps = eA[0][0:1, 0:512]
                sqvr = eB[0][0:1, 0:512]
                P.op("v", lambda e, mrow=mrow: e.tensor_scalar_mul(
                    mrow, ps[0][0:1, :], 1.0 / D),
                    waits=[(cT, snap_st)], reads=("psL0",), writes=("bscr0",), inc=cV)
                P.op("s", lambda e, mrow=mrow, msqr=msqr: e.activation(
                    msqr, mrow, AF.Square),
                    waits=[(cV, cV.count)], reads=("bscr0",), writes=("eA1",), inc=cS)
                P.op("v", lambda e, vrow=vrow, msqr=msqr: e.scalar_tensor_tensor(
                    vrow, ps[1][0:1, :], 1.0 / D, msqr, ALU.mult, ALU.subtract),
                    waits=[(cS, cS.count)], reads=("psL1", "eA1"), writes=("bscr1",),
                    inc=cV)
                P.op("v", lambda e, vrow=vrow, veps=veps: e.tensor_scalar(
                    veps, vrow, 1e-5, None, ALU.add),
                    reads=("bscr1",), writes=("eA0",), inc=cV)
                P.op("s", lambda e, veps=veps, sqvr=sqvr: e.activation(
                    sqvr, veps, AF.Sqrt),
                    waits=[(cV, cV.count)], reads=("eA0",), writes=("eB0",),
                    inc=cS)
                P.op("v", lambda e, rrow=rrow, sqvr=sqvr: e.reciprocal(rrow, sqvr),
                     waits=[(cS, cS.count)], reads=("eB0",), writes=("rscr0",))
                P.op("v", lambda e, mrow=mrow, rrow=rrow, mrrow=mrrow: e.tensor_tensor(
                    mrrow, mrow, rrow, ALU.mult),
                    reads=("bscr0", "rscr0"), writes=("rscr1",), inc=cV)
                rowsnap = cV.count
                P.op("t", lambda e, rrow=rrow: mm(
                    e, ps[2][:], onesrow[:], rrow, True, True),
                    waits=[(cV, rowsnap)], inc=cT)
                P.op("t", lambda e, mrrow=mrrow: mm(
                    e, ps[3][:], onesrow[:], mrrow, True, True), inc=cT)
                bcsnap = cT.count
                for c in range(DC):
                    P.op("v", lambda e, c=c, tsl=tsl: e.tensor_tensor(
                        rscr[0][:], Ur[:, c, tsl], ps[2][:], ALU.mult),
                        waits=[(cT, bcsnap)], reads=("Ur", "psL2"), writes=("rscr0",))
                    P.op("v", lambda e: e.tensor_tensor(
                        rscr[1][:], rscr[0][:], ps[3][:], ALU.subtract),
                        reads=("rscr0", "psL3"), writes=("rscr1",))
                    split_pair(xhi[:, c, tsl], xlo[:, c, tsl], rscr[1][:],
                               "xhi", "xlo", "rscr1", inc=cV)
            snap_x1 = cV.count

            h_mm_snap = [0]
            fc_bank_free = {}
            pj_bank_free = {}
            for fb in range(FB if PH >= 5 else 0):
                for o in range(2):
                    par, _, vsnap = load_strip(
                        wfc[fb, o].rearrange("p c f -> p (c f)"), DC * 128, True)
                    hh = (qhi, qlo)[o]
                    hl = (khi, klo)[o]
                    htag = ("qhi", "qlo")[o]
                    ltag = ("khi", "klo")[o]
                    for t in range(QT):
                        tsl = slice(t * 512, (t + 1) * 512)
                        bi_ = (o * QT + t) % 4
                        bank = ps[bi_]
                        wts = [(cV, vsnap), (cV, snap_x1)]
                        if h_mm_snap[0]:
                            wts.append((cT, h_mm_snap[0]))
                        if bi_ in fc_bank_free:
                            wts.append(fc_bank_free[bi_])
                        first = True
                        for p_ in range(3):
                            lh = (whi, whi, wlo)[p_][par]
                            rh = (xhi, xlo, xhi)[p_]
                            for c in range(DC):
                                last = (p_ == 2 and c == DC - 1)
                                P.op("t", lambda e, lh=lh, rh=rh, c=c, tsl=tsl,
                                     bank=bank, f=first, last=last: mm(
                                         e, bank[:], lh[:, c * 128:(c + 1) * 128],
                                         rh[:, c, tsl], f, last),
                                     waits=wts if first else (),
                                     inc=cT if last else None)
                                first = False
                                wts = ()
                        strip_mm[par] = cT.count
                        mmsnap = cT.count
                        P.op("s", lambda e, bank=bank, o=o: e.activation(
                            rscr[o][:], bank[:], AF.Relu),
                            waits=[(cT, mmsnap), (cV, cV.count)],
                            reads=("psFC",), writes=(f"rscr{o}",))
                        P.op("s", lambda e, hh=hh, tsl=tsl, o=o: e.activation(
                            hh[:, tsl], rscr[o][:], AF.Square),
                            reads=(f"rscr{o}",), writes=(htag,), inc=cS)
                        P.op("v", lambda e, o=o: e.tensor_tensor(
                            bscr[o][:], rscr[o][:], rscr[o][:], ALU.mult),
                            waits=[(cS, cS.count)], reads=(f"rscr{o}",),
                            writes=(f"bscr{o}",))
                        P.op("v", lambda e, hh=hh, hl=hl, tsl=tsl, o=o: e.tensor_tensor(
                            hl[:, tsl], bscr[o][:], hh[:, tsl], ALU.subtract),
                            reads=(f"bscr{o}", htag), writes=(ltag,), inc=cV)
                        fc_bank_free[bi_] = (cS, cS.count)
                fc_done = cV.count
                for o in range(DC if PH >= 5.5 else 0):
                    par, _, vsnap = load_strip(
                        wpr[fb, o].rearrange("p c f -> p (c f)"), 256, True)
                    for t in range(QT):
                        tsl = slice(t * 512, (t + 1) * 512)
                        bi_ = (o * QT + t) % 4
                        bank = ps[4 + bi_]
                        wts = [(cV, vsnap), (cV, fc_done)]
                        if bi_ in pj_bank_free:
                            wts.append(pj_bank_free[bi_])
                        first = True
                        for p_ in range(3):
                            lh = (whi, whi, wlo)[p_][par]
                            for fc_ in range(2):
                                rhh = ((qhi, qlo)[fc_], (khi, klo)[fc_],
                                       (qhi, qlo)[fc_])[p_]
                                last = (p_ == 2 and fc_ == 1)
                                P.op("t", lambda e, lh=lh, rhh=rhh, fc_=fc_, tsl=tsl,
                                     bank=bank, f=first, last=last: mm(
                                         e, bank[:], lh[:, fc_ * 128:(fc_ + 1) * 128],
                                         rhh[:, tsl], f, last),
                                     waits=wts if first else (),
                                     inc=cT if last else None)
                                first = False
                                wts = ()
                        strip_mm[par] = cT.count
                        mmsnap = cT.count
                        if fb == 0:
                            P.op("v", lambda e, bank=bank, o=o, tsl=tsl: e.tensor_copy(
                                Ur[:, o, tsl], bank[:]),
                                waits=[(cT, mmsnap)], reads=("psPJ",), writes=("Ur",),
                                inc=cV)
                        else:
                            P.op("v", lambda e, bank=bank, o=o, tsl=tsl:
                                 e.tensor_tensor(Ur[:, o, tsl], Ur[:, o, tsl],
                                                 bank[:], ALU.add),
                                 waits=[(cT, mmsnap)], reads=("psPJ", "Ur"),
                                 writes=("Ur",), inc=cV)
                        pj_bank_free[bi_] = (cV, cV.count)
                h_mm_snap[0] = cT.count
            snap_y = cV.count
            if dbg and b == 0:
                cDB = dbg_chan[0]
                P.op("g", lambda e: e.dma_start(out=d_y[:], in_=Ur[:]),
                     waits=[(cV, cV.count)], inc=cDB, inc_amt=16)
                P.op("v", lambda e: e.engine_nop(), waits=[(cDB, cDB.count)],
                     writes=("Ur",))

            for o in range(DC if PH >= 6 else 0):
                for t4 in range(QT):
                    tsl = slice(t4 * 512, (t4 + 1) * 512)
                    y_ = Ur[:, o, tsl]
                    P.op("v", lambda e, y_=y_: e.tensor_scalar(
                        bscr[1][:], y_, 1.0, None, ALU.is_gt),
                        waits=[(cV, snap_y)], reads=("Ur",), writes=("bscr1",))
                    for st in range(2, 6):
                        P.op("v", lambda e, y_=y_, st=st: e.scalar_tensor_tensor(
                            rscr[0][:], (y_ if st == 2 else rscr[1][:]), 0.9, y_,
                            ALU.mult, ALU.add),
                            reads=(("Ur",) if st == 2 else ("rscr1", "Ur")),
                            writes=("rscr0",))
                        rtile = bscr[1] if st == 2 else bscr[0]
                        rtag = "bscr1" if st == 2 else "bscr0"
                        P.op("v", lambda e, rtile=rtile: e.tensor_tensor(
                            rscr[1][:], rscr[0][:], rtile[:], ALU.subtract),
                            reads=("rscr0", rtag), writes=("rscr1",))
                        P.op("v", lambda e: e.tensor_scalar(
                            bscr[0][:], rscr[1][:], 1.0, None, ALU.is_gt),
                            reads=("rscr1",), writes=("bscr0",))
                        P.op("v", lambda e: e.tensor_tensor(
                            bscr[1][:], bscr[1][:], bscr[0][:], ALU.add),
                            reads=("bscr0", "bscr1"), writes=("bscr1",))
                    P.op("v", lambda e, o=o, tsl=tsl: e.scalar_tensor_tensor(
                        rscr[0][:], bscr[1][:], 0.2, xhi[:, o, tsl],
                        ALU.mult, ALU.add),
                        reads=("bscr1", "xhi"), writes=("rscr0",))
                    P.op("v", lambda e, o=o, tsl=tsl, y_=y_: e.tensor_tensor(
                        y_, rscr[0][:], xlo[:, o, tsl], ALU.add),
                        reads=("rscr0", "xlo"), writes=("Ur",), inc=cV)
            snap_res2 = cV.count

            for tb in range(TB if PH >= 7 else 0):
                par = tb % 2
                csl = slice(tb * 128, (tb + 1) * 128)
                for c in range(DC):
                    bank = ps[c % 2]
                    P.op("t", lambda e, bank=bank, c=c, csl=csl: e.transpose(
                        bank[:, 0:128], Ur[:, c, csl], c_ident[:]),
                        waits=[(cV, snap_res2), (cV, cV.count),
                               (cDC, snap_consts)], inc=cT)
                    P.op("v", lambda e, bank=bank, c=c, par=par: e.tensor_copy(
                        tm[par][:, c * 128:(c + 1) * 128], bank[:, 0:128]),
                        waits=[(cT, cT.count)], reads=("psT2",), writes=(f"tm{par}",),
                        inc=cV)
                P.op("v", lambda e, par=par: e.tensor_reduce(
                    nsum[:], tm[par][:], AX.X, ALU.add, negate=True),
                    reads=(f"tm{par}",), writes=("nsum",))
                P.op("v", lambda e: e.tensor_scalar_mul(negm[:], nsum[:], 1.0 / D),
                     reads=("nsum",), writes=("negm",), inc=cV)
                P.op("s", lambda e, par=par: e.activation(
                    tm[1 - par][:], tm[par][:], AF.Square, bias=negm[:],
                    accum_out=ssq[:]),
                    waits=[(cV, cV.count)], reads=(f"tm{par}", "negm"),
                    writes=(f"tm{1 - par}", "ssq"), inc=cS)
                P.op("v", lambda e: e.tensor_scalar(
                    vepsc[:], ssq[:], 1.0 / D, 1e-5, ALU.mult, ALU.add),
                    waits=[(cS, cS.count)], reads=("ssq",), writes=("vepsc",), inc=cV)
                P.op("s", lambda e: e.activation(sqv[:], vepsc[:], AF.Sqrt),
                     waits=[(cV, cV.count)], reads=("vepsc",), writes=("sqv",), inc=cS)
                P.op("v", lambda e: e.reciprocal(r2c[:], sqv[:]),
                     waits=[(cS, cS.count)], reads=("sqv",), writes=("r2c",))
                P.op("v", lambda e: e.tensor_scalar_mul(m2c[:], negm[:], -1.0),
                     reads=("negm",), writes=("m2c",), inc=cV)
                P.op("v", lambda e, par=par: e.tensor_scalar(
                    ob[0][:], tm[par][:], m2c[:], r2c[:], ALU.subtract, ALU.mult),
                    waits=[(cO[0], cO[0].count)],
                    reads=(f"tm{par}", "m2c", "r2c"), writes=("ob0",), inc=cV)
                obsnap = cV.count
                P.op("g", lambda e, tb=tb, bcol=bcol: e.dma_start(
                    out=out[bcol + tb * 128: bcol + (tb + 1) * 128, :],
                    in_=ob[0][:]),
                    waits=[(cV, obsnap)], inc=cO[0], inc_amt=16)

        total_out = [cO[0].count, cO[1].count]
        assert cO[1].count == 0 or True
        with nc.Block() as block:
            @block.tensor
            def _(eng):
                P.emit("t", eng)

            @block.vector
            def _(eng):
                P.emit("v", eng)

            @block.scalar
            def _(eng):
                P.emit("s", eng)

            @block.sync
            def _(eng):
                P.emit("y", eng)

            @block.gpsimd
            def _(eng):
                P.emit("g", eng)
                eng.wait_ge(cO[0].sem, total_out[0])
                eng.wait_ge(cO[1].sem, total_out[1])

    return nc


def kernel(L0_attn_x, L0_attn_c, L0_attn_Wq, L0_attn_bq, L0_attn_Wk, L0_attn_bk,
           L0_attn_Wv, L0_attn_bv, L0_attn_Wo, L0_attn_bo, L0_attn_rel_table,
           L0_ffn_W_fc, L0_ffn_W_proj):
    x = np.ascontiguousarray(np.asarray(L0_attn_x, dtype=np.float32))
    B, S, D = x.shape
    H = 16
    F = np.asarray(L0_ffn_W_fc).shape[0]
    NB = B // N_CORES
    cfg = make_cfg(S=S, D=D, H=H, F=F, NB=NB)

    rel = np.asarray(L0_attn_rel_table, np.float32)
    shared = dict(
        wq=pack_weight_strips_qkv(np.asarray(L0_attn_Wq, np.float32), D),
        wk=pack_weight_strips_qkv(np.asarray(L0_attn_Wk, np.float32), D),
        wv=pack_weight_strips_qkv(np.asarray(L0_attn_Wv, np.float32), D),
        wo=pack_weight_strips_qkv(np.asarray(L0_attn_Wo, np.float32), D),
        wfc=pack_wfc_strips(np.asarray(L0_ffn_W_fc, np.float32), D, F),
        wpr=pack_wpr_strips(np.asarray(L0_ffn_W_proj, np.float32), D, F),
        bqc=pack_bias_cols(np.asarray(L0_attn_bq, np.float32), D),
        bkc=pack_bias_cols(np.asarray(L0_attn_bk, np.float32), D),
        bvc=pack_bias_cols(np.asarray(L0_attn_bv, np.float32), D),
        boc=pack_bias_cols(np.asarray(L0_attn_bo, np.float32), D),
        crep=np.ascontiguousarray(
            np.repeat(rel[cfg["CLIP"]:cfg["CLIP"] + 1, :], 128, 0)),
        bands=pack_bands(rel, cfg),
        ident=np.eye(128, dtype=np.float32),
    )
    in_maps = []
    for core in range(N_CORES):
        xc = x[core * NB:(core + 1) * NB]
        m = dict(shared)
        m["xT"] = np.ascontiguousarray(xc.transpose(2, 0, 1).reshape(D, NB * S))
        in_maps.append(m)

    nc = build_program(cfg)
    res = run_bass_kernel_spmd(nc, in_maps, core_ids=list(range(N_CORES)))
    outs = [np.asarray(r["out"]).reshape(NB, S, D) for r in res.results]
    return np.ascontiguousarray(np.concatenate(outs, axis=0))
